# version 15
# baseline (speedup 1.0000x reference)
import math
from contextlib import ExitStack

import numpy as np
import ml_dtypes

import concourse.bass as bass
import concourse.mybir as mybir
from concourse.bass_utils import run_bass_kernel_spmd

F32 = mybir.dt.float32
BF16 = mybir.dt.bfloat16
AF = mybir.ActivationFunctionType
ALU = mybir.AluOpType
AX = mybir.AxisListType

NCORES = 8
D = 2048
T = 1024
NH = 16
INTER = 5632
ICN = INTER // 128
DCN = D // 128
EPS = 1e-5
RSQ = 1.0 / math.sqrt(128.0)
QUAD_ORDER = ((0, 0), (0, 1), (1, 0), (1, 1))

ENGINES = ("sync", "scalar", "vector", "tensor", "gpsimd")


class Plan:

    def __init__(self):
        self.ops = {e: [] for e in ENGINES}
        self.sems = {}
        self.marks = {}

    def add_sem(self, name, handle):
        self.sems[name] = [handle, 0]

    def cnt(self, name):
        return self.sems[name][1]

    def emit(self, eng, fn):
        self.ops[eng].append(fn)

    def wait(self, eng, sname, val):
        if val is None or val <= 0:
            return
        h, n = self.sems[sname]
        assert val <= n, f"wait {sname}>={val} but only {n} planned (eng={eng})"
        key = (eng, sname)
        if self.marks.get(key, 0) >= val:
            return
        self.marks[key] = val
        self.emit(eng, lambda e, h=h, v=val: e.wait_ge(h, v))

    def _inc(self, sname, k):
        rec = self.sems[sname]
        rec[1] += k
        return rec[0], k

    def op(self, eng, meth, *args, inc=None, inck=1, **kw):
        tup = self._inc(inc, inck) if inc else None

        def f(e, meth=meth, args=args, kw=kw, tup=tup):
            ins = getattr(e, meth)(*args, **kw)
            if tup is not None:
                ins.then_inc(tup[0], tup[1])

        self.emit(eng, f)
        return self.sems[inc][1] if inc else None

    def dma(self, eng, dst, src, inc="sY", **kw):
        return self.op(eng, "dma_start", dst, src, inc=inc, inck=16, **kw)


def build(phase=3, dbg=False):
    nc = bass.Bass("TRN2", target_bir_lowering=False)

    x_in = nc.dram_tensor("x_in", [D, T], F32, kind="ExternalInput")
    wq_d = nc.dram_tensor("wq", [D, D], F32, kind="ExternalInput")
    wk_d = nc.dram_tensor("wk", [128, D], F32, kind="ExternalInput")
    wv_d = nc.dram_tensor("wv", [128, D], F32, kind="ExternalInput")
    wo_d = nc.dram_tensor("wo", [D, D], F32, kind="ExternalInput")
    gw_d = nc.dram_tensor("gw", [ICN * 128, D], BF16, kind="ExternalInput")
    uw_d = nc.dram_tensor("uw", [ICN * 128, D], BF16, kind="ExternalInput")
    dw_d = nc.dram_tensor("dw", [2 * ICN * 128, T], BF16, kind="ExternalInput")
    bq_d = nc.dram_tensor("bq_l", [128, 16], F32, kind="ExternalInput")
    bk_d = nc.dram_tensor("bk_c", [128, 1], F32, kind="ExternalInput")
    bv_d = nc.dram_tensor("bv_r", [1, 128], F32, kind="ExternalInput")
    bo_d = nc.dram_tensor("bo_l", [128, 16], F32, kind="ExternalInput")
    gs_d = nc.dram_tensor("gs_l", [128, ICN], F32, kind="ExternalInput")
    us_d = nc.dram_tensor("us_l", [128, ICN], F32, kind="ExternalInput")
    ds_d = nc.dram_tensor("ds_l", [128, 16], F32, kind="ExternalInput")
    y_out = nc.dram_tensor("y_out", [D, T], F32, kind="ExternalOutput")
    if dbg:
        dbg_ln = [nc.dram_tensor(f"dbg_ln{i}", [D, T], F32, kind="ExternalOutput")
                  for i in (1, 2, 3)]
        dbg_x2 = nc.dram_tensor("dbg_x2", [D, T], F32, kind="ExternalOutput")
        dbg_bnd = nc.dram_tensor("dbg_bnd", [16, 4], F32, kind="ExternalOutput")
        dbg_sum = nc.dram_tensor("dbg_sum", [16, T], F32, kind="ExternalOutput")
        dbg_rcp = nc.dram_tensor("dbg_rcp", [16, T], F32, kind="ExternalOutput")
        dbg_gsf = nc.dram_tensor("dbg_gsf", [128, T], F32, kind="ExternalOutput")
        dbg_rcf = nc.dram_tensor("dbg_rcf", [128, T], F32, kind="ExternalOutput")
        dbg_ones = nc.dram_tensor("dbg_ones", [128, 128], BF16,
                                  kind="ExternalOutput")
        dbg_h = nc.dram_tensor("dbg_h", [ICN * 128, T], F32, kind="ExternalOutput")
        dbg_ab = nc.dram_tensor("dbg_ab", [6, T], F32, kind="ExternalOutput")
    if phase >= 2:
        kv_in = nc.dram_tensor("kv_in", [256, T], BF16, kind="Internal")
        kv_gat = nc.dram_tensor("kv_gat", [512, T], BF16, kind="Internal")

    BASE = 16512
    A = lambda name, shape, dtype, offset: nc.alloc_sbuf_tensor_at(
        name, shape, dtype, offset=BASE + offset)
    x_sb = A("x_sb", [128, DCN * T], F32, offset=0)
    LNO = 65536
    ln_sb = A("ln_sb", [128, DCN * T], BF16, offset=LNO)
    HO = 98304
    h_sb = A("h_sb", [128, ICN * T], BF16, offset=HO)
    wog = A("wog", [128, 4 * 2048], BF16, offset=HO + 0)
    wqb = [A(f"wqb{i}", [128, 2048], BF16, offset=HO + 16384 + i * 4096) for i in range(2)]
    qsb = [A(f"qsb{i}", [128, T], BF16, offset=HO + 24576 + i * 2048) for i in range(2)]
    pt_full = A("pt_full", [128, 16 * 1024], BF16, offset=HO + 28672)
    xb16_b = A("xb16_b", [128, DCN * T], BF16, offset=HO + 28672)
    oh = [A(f"oh{i}", [128, 4 * T], BF16, offset=HO + 61440 + i * 8192) for i in range(2)]
    rowA = A("rowA", [1, T], F32, offset=HO + 77824)
    rowB = A("rowB", [1, T], F32, offset=HO + 81920)
    rowT = A("rowT", [1, T], F32, offset=HO + 86016)
    xb16_a = A("xb16_a", [128, DCN * T], BF16, offset=HO + 0)
    R1 = 188416
    gsl = [A(f"gsl{i}", [128, 2048], BF16, offset=R1 + i * 4096) for i in range(2)]
    usl = [A(f"usl{i}", [128, 2048], BF16, offset=R1 + 8192 + i * 4096) for i in range(2)]
    gs_sb = A("gs_sb", [128, T], F32, offset=R1 + 16384)
    dsl = [A(f"dsl{j}", [128, T], BF16, offset=R1 + j * 4096) for j in range(4)]
    ksb = A("ksb", [128, 2048], BF16, offset=R1 + 0)
    vsb = A("vsb", [128, 2048], BF16, offset=R1 + 4096)
    wkb = A("wkb", [128, 2048], BF16, offset=R1 + 8192)
    wvb = A("wvb", [128, 2048], BF16, offset=R1 + 12288)
    recip_sb = A("recip_sb", [128, 1024], F32, offset=R1 + 8192)
    ksq = A("ksq", [128, 2048], BF16, offset=R1 + 12288)
    qsq = A("qsq", [128, 1024], BF16, offset=R1 + 12288)
    negrow = [A(f"negrow{i}", [1, 1024], BF16, offset=HO + 86016 + i * 2048)
              for i in range(2)]
    SM = 208896
    bqs = A("bqs", [128, 16], F32, offset=SM)
    bkc = A("bkc", [128, 1], F32, offset=SM + 64)
    bol = A("bol", [128, 16], F32, offset=SM + 96)
    gss = A("gss", [128, ICN], F32, offset=SM + 160)
    uss = A("uss", [128, ICN], F32, offset=SM + 352)
    dsh = A("dsh", [128, 16], F32, offset=SM + 544)
    bql_raw = A("bql_raw", [128, 16], F32, offset=SM + 608)
    dsl_raw = A("dsl_raw", [128, 16], F32, offset=SM + 672)
    scol = A("scol", [128, 8], F32, offset=SM + 736)
    ones_cb = A("ones_cb", [128, 1], BF16, offset=SM + 768)
    ones128 = A("ones128", [128, 128], BF16, offset=SM + 2368)
    onesf_r = A("onesf_r", [1, 128], F32, offset=SM + 800)
    onesb_r = A("onesb_r", [1, 128], BF16, offset=SM + 1312)
    bvr_f = A("bvr_f", [1, 128], F32, offset=SM + 1568)
    bvr_b = A("bvr_b", [1, 128], BF16, offset=SM + 2080)
    eps_c = A("eps_c", [1, 1], F32, offset=SM + 2336)
    eps2_c = A("eps2_c", [128, 1], F32, offset=SM + 2624)

    PSB = nc.alloc_psum_tensor("PSB", [128, 2048], F32)
    PS2 = nc.alloc_psum_tensor("PS2", [128, 1024], F32)
    PS3 = nc.alloc_psum_tensor("PS3", [128, 1024], F32)

    P = Plan()
    ctx = ExitStack()
    for s in ("sY", "sG", "sC", "sP", "sV", "sA", "sM",
              "sS0", "sS1", "sS2", "sS3", "sH0", "sH1", "sWO", "sKA", "sKB",
              "sPR0", "sPR1", "sPR2", "sDB"):
        P.add_sem(s, ctx.enter_context(nc.semaphore(s)))
    M = {}

    P.op("gpsimd", "memset", ones_cb[:, :], 1.0)
    P.op("gpsimd", "memset", onesf_r[:, :], 1.0)
    P.op("gpsimd", "memset", onesb_r[:, :], 1.0)
    P.op("gpsimd", "memset", ones128[:, :], 1.0)
    P.op("gpsimd", "memset", eps_c[:, :], EPS)
    P.op("gpsimd", "memset", eps2_c[:, :], 1e-30, inc="sM")
    for dst, src in ((bql_raw, bq_d), (bkc, bk_d), (bvr_f, bv_d), (bol, bo_d),
                     (gss, gs_d), (uss, us_d), (dsl_raw, ds_d)):
        P.dma("gpsimd", dst[:, :], src[:, :], "sG")
    M["setup_dmas"] = P.cnt("sG")
    P.wait("gpsimd", "sG", M["setup_dmas"])
    P.op("gpsimd", "tensor_copy", bvr_b[:, :], bvr_f[:, :], inc="sM")
    M["setup_gp"] = P.cnt("sM")
    P.wait("vector", "sG", M["setup_dmas"])
    P.op("vector", "tensor_scalar", bqs[:, :], bql_raw[:, :], RSQ, None, ALU.mult)
    P.op("vector", "tensor_scalar", dsh[:, :], dsl_raw[:, :], 0.5, None, ALU.mult,
         inc="sV")
    M["setup_v"] = P.cnt("sV")
    P.wait("vector", "sV", M["setup_v"])

    for c in range(DCN):
        P.dma("sync", x_sb[:, c * T:(c + 1) * T], x_in[c * 128:(c + 1) * 128, :])
    M["x_loaded"] = P.cnt("sY")

    def layernorm(tag, xb16, pre_dve=(), pre_pe=()):
        for sn, v in pre_dve:
            P.wait("vector", sn, v)
        if dbg and tag != "ln1":
            P.wait("vector", "sDB", P.cnt("sDB"))
        if tag == "ln1":
            P.wait("vector", "sY", M["x_loaded"])
        P.wait("vector", "sV", P.cnt("sV"))
        dve_done = []
        for c in range(DCN):
            P.op("vector", "tensor_copy", xb16[:, c * T:(c + 1) * T],
                 x_sb[:, c * T:(c + 1) * T], inc="sV")
            P.wait("vector", "sV", P.cnt("sV"))
            P.op("vector", "tensor_tensor", ln_sb[:, c * T:(c + 1) * T],
                 xb16[:, c * T:(c + 1) * T], xb16[:, c * T:(c + 1) * T], ALU.mult,
                 inc="sV")
            dve_done.append(P.cnt("sV"))
        for sn, v in pre_pe:
            P.wait("tensor", sn, v)
        P.wait("tensor", "sM", M["setup_gp"])
        for c in range(DCN):
            P.wait("tensor", "sV", dve_done[c])
            st, sp = (c == 0), (c == DCN - 1)
            for hf in range(2):
                s0 = c * T + hf * 512
                P.op("tensor", "matmul", PSB[:, hf * 512:hf * 512 + 512],
                     ones128[:, :], xb16[:, s0:s0 + 512], start=st, stop=sp)
                P.op("tensor", "matmul", PSB[:, 1024 + hf * 512:1536 + hf * 512],
                     ones128[:, :], ln_sb[:, s0:s0 + 512], start=st, stop=sp,
                     inc="sP" if (sp and hf == 1) else None)
        M[tag + "_stats"] = P.cnt("sP")
        P.wait("vector", "sP", M[tag + "_stats"])
        P.op("vector", "tensor_scalar", rowB[:, :], PSB[0:1, 0:1024], 1.0 / D,
             None, ALU.mult, inc="sV")
        P.op("vector", "tensor_scalar", rowA[:, :], PSB[0:1, 1024:2048], 1.0 / D,
             None, ALU.mult, inc="sV")
        P.wait("vector", "sV", P.cnt("sV"))
        P.op("vector", "tensor_tensor", rowT[:, :], rowB[:, :], rowB[:, :],
             ALU.mult, inc="sV")
        P.wait("vector", "sV", P.cnt("sV"))
        P.op("vector", "tensor_tensor", rowA[:, :], rowA[:, :], rowT[:, :],
             ALU.subtract, inc="sV")
        M[tag + "_var"] = P.cnt("sV")
        P.wait("scalar", "sV", M[tag + "_var"])
        P.wait("scalar", "sM", M["setup_gp"])
        P.op("scalar", "activation", rowT[:, :], rowA[:, :], AF.Ln,
             eps_c[0:1, 0:1], 1.0, inc="sA")
        P.wait("scalar", "sA", P.cnt("sA"))
        P.op("scalar", "activation", rowA[:, :], rowT[:, :], AF.Exp,
             0.0, -0.5, inc="sA")
        M[tag + "_sqrt"] = P.cnt("sA")
        P.wait("vector", "sA", M[tag + "_sqrt"])
        P.op("vector", "tensor_tensor", rowB[:, :], rowB[:, :], rowA[:, :],
             ALU.mult, inc="sV")
        M[tag + "_ab"] = P.cnt("sV")
        P.wait("tensor", "sV", M[tag + "_ab"])
        for hf in range(2):
            sl = slice(hf * 512, hf * 512 + 512)
            P.op("tensor", "matmul", PS2[:, sl], onesf_r[:, :], rowA[0:1, sl],
                 start=True, stop=True)
            P.op("tensor", "matmul", PS3[:, sl], onesf_r[:, :], rowB[0:1, sl],
                 start=True, stop=True, inc="sP" if hf == 1 else None)
        M[tag + "_bcast"] = P.cnt("sP")
        P.wait("vector", "sP", M[tag + "_bcast"])
        for c in range(DCN):
            P.wait("vector", "sV", P.cnt("sV"))
            P.op("vector", "tensor_tensor", gs_sb[:, :], x_sb[:, c * T:(c + 1) * T],
                 PS2[:, :], ALU.mult, inc="sV")
            P.wait("vector", "sV", P.cnt("sV"))
            P.op("vector", "tensor_tensor", ln_sb[:, c * T:(c + 1) * T],
                 gs_sb[:, :], PS3[:, :], ALU.subtract, inc="sV")
            M.setdefault(tag + "_applyc", []).append(P.cnt("sV"))
        M[tag + "_apply"] = P.cnt("sV")
        if dbg:
            i = {"ln1": 0, "ln2": 1, "ln3": 2}[tag]
            P.wait("gpsimd", "sV", M[tag + "_apply"])
            for c in range(DCN):
                P.dma("gpsimd", dbg_ln[i][c * 128:(c + 1) * 128, :],
                      ln_sb[:, c * T:(c + 1) * T], "sDB")
            P.dma("gpsimd", dbg_ab[2 * i:2 * i + 1, :], rowA[0:1, :], "sDB")
            P.dma("gpsimd", dbg_ab[2 * i + 1:2 * i + 2, :], rowB[0:1, :], "sDB")
            M[tag + "_dump"] = P.cnt("sDB")

    def ffn(f, ln_tag):
        g_ps = (PSB[:, 0:1024], PS2[:, :])
        u_ps = (PSB[:, 1024:2048], PS3[:, :])
        slab_loaded, g_done, u_done, silu_done, hmul_done = {}, {}, {}, {}, {}

        def emit_slab(ic):
            sg, su = f"sS{ic % 2}", f"sS{2 + ic % 2}"
            if ic >= 2:
                P.wait("sync", "sP", g_done[ic - 2])
            elif f == 1:
                P.wait("sync", "sP", M["wo_pe_done"])
                for j in range(3):
                    P.wait("sync", f"sPR{j}", P.cnt(f"sPR{j}"))
            P.dma("sync", gsl[ic % 2][:, :], gw_d[ic * 128:(ic + 1) * 128, :], sg)
            if ic >= 2:
                P.wait("sync", "sP", u_done[ic - 2])
            P.dma("sync", usl[ic % 2][:, :], uw_d[ic * 128:(ic + 1) * 128, :], su)
            slab_loaded[ic] = ((sg, P.cnt(sg)), (su, P.cnt(su)))

        emit_slab(0)
        emit_slab(1)
        for ic in range(ICN):
            gp, up = g_ps[ic % 2], u_ps[ic % 2]
            for sn, v in slab_loaded[ic]:
                P.wait("tensor", sn, v)
            if ic == 1:
                P.wait("tensor", "sV", M[ln_tag + "_apply"])
            if ic >= 2:
                P.wait("tensor", "sA", silu_done[ic - 2])
            for hc in range(DCN):
                if ic == 0:
                    P.wait("tensor", "sV", M[ln_tag + "_applyc"][hc])
                for hf in range(2):
                    s0 = hc * T + hf * 512
                    P.op("tensor", "matmul", gp[:, hf * 512:hf * 512 + 512],
                         gsl[ic % 2][:, hc * 128:(hc + 1) * 128],
                         ln_sb[:, s0:s0 + 512], start=(hc == 0),
                         stop=(hc == DCN - 1),
                         inc="sP" if (hc == DCN - 1 and hf == 1) else None)
            g_done[ic] = P.cnt("sP")
            if ic >= 2:
                P.wait("tensor", "sV", hmul_done[ic - 2])
            for hc in range(DCN):
                for hf in range(2):
                    s0 = hc * T + hf * 512
                    P.op("tensor", "matmul", up[:, hf * 512:hf * 512 + 512],
                         usl[ic % 2][:, hc * 128:(hc + 1) * 128],
                         ln_sb[:, s0:s0 + 512], start=(hc == 0),
                         stop=(hc == DCN - 1),
                         inc="sP" if (hc == DCN - 1 and hf == 1) else None)
            u_done[ic] = P.cnt("sP")
            P.wait("scalar", "sP", g_done[ic])
            if ic >= 1:
                P.wait("scalar", "sV", hmul_done[ic - 1])
            P.op("scalar", "activation", gs_sb[:, :], gp[:, :], AF.Silu, 0.0,
                 gss[:, ic:ic + 1], inc="sA")
            silu_done[ic] = P.cnt("sA")
            P.wait("vector", "sP", u_done[ic])
            P.wait("vector", "sA", silu_done[ic])
            if ic == 0 and "dbg_gate" in M:
                P.wait("vector", *M["dbg_gate"])
            P.op("vector", "scalar_tensor_tensor", h_sb[:, ic * T:(ic + 1) * T],
                 up[:, :], uss[:, ic:ic + 1], gs_sb[:, :], ALU.mult, ALU.mult,
                 inc="sV")
            hmul_done[ic] = P.cnt("sV")
            if ic + 2 < ICN:
                emit_slab(ic + 2)

        def bank(dc8):
            if dc8 < 4:
                return PSB[:, dc8 * 512:dc8 * 512 + 512]
            t = PS2 if dc8 < 6 else PS3
            return t[:, (dc8 % 2) * 512:(dc8 % 2) * 512 + 512]

        dwn_loaded, ic_pe = {}, {}
        epi = {qq: {} for qq in range(4)}
        for qq, (dh, th) in enumerate(QUAD_ORDER):
            def emit_dslab(ic, qq=qq, dh=dh):
                slot = ic % 4
                if qq == 0 and ic < 4:
                    P.wait("sync", "sP",
                           (g_done if slot < 2 else u_done)[ICN - 2 + slot % 2])
                elif ic < 4:
                    P.wait("sync", "sP", ic_pe[(qq - 1, ICN - 4 + slot)])
                else:
                    P.wait("sync", "sP", ic_pe[(qq, ic - 4)])
                r0 = (dh * ICN + ic) * 128
                P.dma("sync", dsl[slot][:, :], dw_d[r0:r0 + 128, :], f"sS{slot}")
                dwn_loaded[(qq, ic)] = (f"sS{slot}", P.cnt(f"sS{slot}"))

            for ic in range(4):
                emit_dslab(ic)
            for ic in range(ICN):
                P.wait("tensor", *dwn_loaded[(qq, ic)])
                if ic == 0 and qq == 0:
                    P.wait("tensor", "sV", hmul_done[ICN - 1])
                for dc8 in range(8):
                    if ic == 0 and qq > 0:
                        P.wait("tensor", "sV", epi[qq - 1][dc8])
                    P.op("tensor", "matmul", bank(dc8),
                         dsl[ic % 4][:, dc8 * 128:(dc8 + 1) * 128],
                         h_sb[:, ic * T + th * 512: ic * T + th * 512 + 512],
                         start=(ic == 0), stop=(ic == ICN - 1),
                         inc="sP" if dc8 == 7 else None)
                ic_pe[(qq, ic)] = P.cnt("sP")
                if ic + 4 < ICN:
                    emit_dslab(ic + 4)
            P.wait("vector", "sP", ic_pe[(qq, ICN - 1)])
            for dc8 in range(8):
                dc = dh * 8 + dc8
                xv = x_sb[:, dc * T + th * 512: dc * T + th * 512 + 512]
                P.op("vector", "scalar_tensor_tensor", xv, bank(dc8),
                     dsh[:, dc:dc + 1], xv, ALU.mult, ALU.add, inc="sV")
                epi[qq][dc8] = P.cnt("sV")
        M[f"ffn{f}_pe_done"] = P.cnt("sP")
        M[f"ffn{f}_x_done"] = P.cnt("sV")
        if dbg and f == 1:
            P.wait("gpsimd", "sV", hmul_done[ICN - 1])
            for ic in range(ICN):
                P.dma("gpsimd", dbg_h[ic * 128:(ic + 1) * 128, :],
                      h_sb[:, ic * T:(ic + 1) * T], "sDB")

    def attention():
        layernorm("ln2", xb16_b,
                  pre_dve=(("sP", M["ffn0_pe_done"]),),
                  pre_pe=(("sV", M["ffn0_x_done"]),))
        P.wait("gpsimd", "sP", M["ffn0_pe_done"])
        P.dma("gpsimd", wkb[:, :], wk_d[:, :], "sKA")
        P.dma("gpsimd", wvb[:, :], wv_d[:, :], "sKB")
        M["wkv_dma"] = (P.cnt("sKA"), P.cnt("sKB"))
        P.wait("tensor", "sKA", M["wkv_dma"][0])
        P.wait("tensor", "sKB", M["wkv_dma"][1])
        for hc in range(DCN):
            P.wait("tensor", "sV", M["ln2_applyc"][hc])
            for hf in range(2):
                P.op("tensor", "matmul", PSB[:, hf * 512:hf * 512 + 512],
                     wkb[:, hc * 128:(hc + 1) * 128],
                     ln_sb[:, hc * T + hf * 512: hc * T + hf * 512 + 512],
                     start=(hc == 0), stop=(hc == DCN - 1),
                     inc="sP" if (hc == DCN - 1 and hf == 1) else None)
        M["k_pe"] = P.cnt("sP")
        for tc in range(8):
            for hc in range(DCN):
                P.op("tensor", "matmul", PSB[:, 1024 + tc * 128:1152 + tc * 128],
                     ln_sb[:, hc * T + tc * 128: hc * T + tc * 128 + 128],
                     wvb[:, hc * 128:(hc + 1) * 128], start=(hc == 0), stop=False)
            P.op("tensor", "matmul", PSB[:, 1024 + tc * 128:1152 + tc * 128],
                 onesb_r[:, :], bvr_b[:, :], start=False, stop=True,
                 inc="sP" if tc == 7 else None)
        M["v_pe"] = P.cnt("sP")
        P.wait("vector", "sP", M["k_pe"])
        P.op("vector", "tensor_scalar", ksb[:, 0:1024], PSB[:, 0:1024], bkc[:, 0:1],
             None, ALU.add, inc="sV")
        P.wait("vector", "sP", M["v_pe"])
        P.op("vector", "tensor_copy", vsb[:, 0:1024], PSB[:, 1024:2048], inc="sV")
        M["kv_evac"] = P.cnt("sV")
        P.wait("gpsimd", "sV", M["kv_evac"])
        P.dma("gpsimd", kv_in[0:128, :], ksb[:, 0:1024], "sG")
        P.dma("gpsimd", kv_in[128:256, :], vsb[:, 0:1024], "sG")
        P.wait("gpsimd", "sG", P.cnt("sG"))
        P.op("gpsimd", "collective_compute", "AllGather", ALU.bypass,
             [[0, 1], [2, 3], [4, 5], [6, 7]],
             [kv_in.ap().opt()], [kv_gat.ap().opt()], inc="sC")
        P.wait("gpsimd", "sC", P.cnt("sC"))
        for r in range(2):
            P.dma("gpsimd", ksb[:, r * 1024:(r + 1) * 1024],
                  kv_gat[r * 256:r * 256 + 128, :], "sG")
            P.dma("gpsimd", vsb[:, r * 1024:(r + 1) * 1024],
                  kv_gat[r * 256 + 128:r * 256 + 256, :], "sG")
        M["kv_gathered"] = P.cnt("sG")

        P.wait("vector", "sG", M["kv_gathered"])
        P.wait("vector", "sP", M["v_pe"])
        P.op("vector", "tensor_tensor", ksq[:, :], ksb[:, :], ksb[:, :],
             ALU.mult, inc="sV")
        M["ksq"] = P.cnt("sV")
        P.wait("tensor", "sV", M["ksq"])
        P.wait("tensor", "sV", M["kv_evac"])
        for kk in range(4):
            P.op("tensor", "matmul", PSB[:, kk * 512:(kk + 1) * 512],
                 ones128[:, :], ksq[:, kk * 512:(kk + 1) * 512],
                 start=True, stop=True, inc="sP" if kk == 3 else None)
        M["kn2_pe"] = P.cnt("sP")
        P.wait("vector", "sP", M["kn2_pe"])
        P.op("vector", "reduce_max", scol[0:1, 0:1], PSB[0:1, 0:2048], AX.X,
             inc="sV")
        P.wait("vector", "sV", P.cnt("sV"))
        M["maxk2v"] = P.cnt("sV")
        P.wait("scalar", "sV", M["maxk2v"])
        P.op("scalar", "activation", scol[0:1, 1:2], scol[0:1, 0:1], AF.Ln,
             0.0, 1.0, inc="sA")
        M["lnmaxk2"] = P.cnt("sA")
        P.wait("vector", "sA", M["lnmaxk2"])
        P.op("vector", "tensor_scalar", scol[0:1, 1:2], scol[0:1, 1:2], 0.5,
             None, ALU.mult, inc="sV")
        P.wait("vector", "sV", P.cnt("sV"))
        M["maxk2"] = P.cnt("sV")

        qdma, qpe, qevac, sqrt_done, negrow_done = {}, {}, {}, {-1: 0}, {}
        exp_at, exp_g, avpe, ohev, pt_free = {}, {}, {}, {}, {}
        wo_pe, wo_evac, wo_dma = {}, {}, {}
        recip_done = {}
        wo_queue = []
        bnd_dump, sum_dump, rcp_dump = {}, {}, {}

        def emit_qdma(hh):
            P.wait("gpsimd", "sP",
                   M["ffn0_pe_done"] if hh < 2 else qpe[hh - 2])
            P.dma("gpsimd", wqb[hh % 2][:, :], wq_d[hh * 128:(hh + 1) * 128, :],
                  f"sH{hh % 2}")
            qdma[hh] = (f"sH{hh % 2}", P.cnt(f"sH{hh % 2}"))

        def emit_qmm(hh):
            P.wait("tensor", *qdma[hh])
            P.wait("tensor", "sV", M["kv_evac"] if hh == 0 else qevac[hh - 1])
            P.wait("tensor", "sA", sqrt_done[hh - 1])
            if hh >= 1:
                P.wait("tensor", "sV", negrow_done[hh - 1])
            for hc in range(DCN):
                for hf in range(2):
                    P.op("tensor", "matmul", PS3[:, hf * 512:hf * 512 + 512],
                         wqb[hh % 2][:, hc * 128:(hc + 1) * 128],
                         ln_sb[:, hc * T + hf * 512: hc * T + hf * 512 + 512],
                         start=(hc == 0), stop=(hc == DCN - 1),
                         inc="sP" if (hc == DCN - 1 and hf == 1) else None)
            qpe[hh] = P.cnt("sP")

        def emit_qfinish(hh):
            P.wait("vector", "sP", qpe[hh])
            if hh >= 2:
                P.wait("vector", "sP", M[f"score_last_{hh - 2}"])
            P.op("vector", "tensor_scalar", qsb[hh % 2][:, :], PS3[:, :], RSQ,
                 bqs[:, hh:hh + 1], ALU.mult, ALU.add, inc="sV")
            qevac[hh] = P.cnt("sV")

        def emit_qbound(hh):
            P.wait("vector", "sV", qevac[hh])
            if hh >= 1:
                P.wait("vector", "sP", M[f"qn2_{hh - 1}"])
            else:
                P.wait("vector", "sP", M["kn2_pe"])
            P.op("vector", "tensor_tensor", qsq[:, :], qsb[hh % 2][:, :],
                 qsb[hh % 2][:, :], ALU.mult, inc="sV")
            sq_a = P.cnt("sV")
            P.wait("tensor", "sV", sq_a)
            for hf in range(2):
                P.op("tensor", "matmul", PS3[:, hf * 512:hf * 512 + 512],
                     ones128[:, :], qsq[:, hf * 512:hf * 512 + 512],
                     start=True, stop=True, inc="sP" if hf == 1 else None)
            M[f"qn2_{hh}"] = P.cnt("sP")
            P.wait("scalar", "sP", M[f"qn2_{hh}"])
            P.wait("scalar", "sV", M["maxk2"])
            if hh >= 1:
                P.wait("scalar", "sV", negrow_done[hh - 1])
            P.op("scalar", "activation", rowB[:, :], PS3[0:1, 0:1024], AF.Ln,
                 0.0, 1.0, inc="sA")
            lnq = P.cnt("sA")
            P.wait("vector", "sA", lnq)
            P.op("vector", "reduce_max", scol[0:1, 2:3], rowB[:, :], AX.X,
                 inc="sV")
            P.wait("vector", "sV", P.cnt("sV"))
            mx = P.cnt("sV")
            P.wait("scalar", "sV", mx)
            P.op("scalar", "activation", scol[0:1, 3:4], scol[0:1, 2:3], AF.Exp,
                 scol[0:1, 1:2], 0.5, inc="sA")
            bexp = P.cnt("sA")
            P.wait("vector", "sA", bexp)
            P.op("vector", "tensor_scalar", scol[0:1, 3:4], scol[0:1, 3:4],
                 -1.0, None, ALU.mult, inc="sV")
            P.wait("vector", "sV", P.cnt("sV"))
            nb = P.cnt("sV")
            P.wait("tensor", "sV", nb)
            P.op("tensor", "matmul", PS3[:, 0:1], onesf_r[:, :],
                 scol[0:1, 3:4], start=True, stop=True, inc="sP")
            nbp = P.cnt("sP")
            P.wait("vector", "sP", nbp)
            if (hh - 2, 31) in exp_at:
                P.wait("vector", "sA", exp_at[(hh - 2, 31)])
            if dbg and (hh - 2) in bnd_dump:
                P.wait("vector", "sDB", bnd_dump[hh - 2])
            P.op("vector", "tensor_copy", scol[:, 4 + hh % 2:5 + hh % 2],
                 PS3[:, 0:1], inc="sV")
            P.wait("vector", "sV", P.cnt("sV"))
            sqrt_done[hh] = bexp
            negrow_done[hh] = P.cnt("sV")
            if dbg:
                P.wait("gpsimd", "sV", negrow_done[hh])
                P.dma("gpsimd", dbg_bnd[hh:hh + 1, 0:1],
                      scol[0:1, 4 + hh % 2:5 + hh % 2], "sDB")
                P.dma("gpsimd", dbg_bnd[hh:hh + 1, 1:2], scol[0:1, 2:3], "sDB")
                P.dma("gpsimd", dbg_bnd[hh:hh + 1, 2:3], scol[0:1, 1:2], "sDB")
                bnd_dump[hh] = P.cnt("sDB")

        def emit_qstage(hh):
            emit_qmm(hh)
            emit_qfinish(hh)
            emit_qbound(hh)

        def wo_bank(g, i):
            if g == 3 and i % 2 == 1:
                return PS3[:, 512:1024]
            return PS2[:, 512:1024]

        def emit_wo_group_mm(g, i):
            dc, hf = i // 2, i % 2
            P.wait("tensor", "sWO", wo_dma[(g, 3)])
            if i == 0:
                P.wait("tensor", "sV", ohev[(4 * g + 3, 1)])
                if g == 3:
                    P.wait("tensor", "sV", qevac[NH - 1])
                    P.wait("tensor", "sA", sqrt_done[NH - 1])
            back = 2 if g == 3 else 1
            if (g, i - back) in wo_evac:
                P.wait("tensor", "sV", wo_evac[(g, i - back)])
            elif g > 0:
                P.wait("tensor", "sV", wo_evac[(g - 1, 31)])
            for k in range(4):
                P.op("tensor", "matmul", wo_bank(g, i),
                     wog[:, k * 2048 + dc * 128: k * 2048 + (dc + 1) * 128],
                     oh[g % 2][:, k * T + hf * 512: k * T + hf * 512 + 512],
                     start=(k == 0), stop=(k == 3),
                     inc="sP" if k == 3 else None)
            wo_pe[(g, i)] = P.cnt("sP")
            P.wait("vector", "sP", wo_pe[(g, i)])
            xv = x_sb[:, dc * T + hf * 512: dc * T + hf * 512 + 512]
            sc = bol[:, dc:dc + 1] if g == 0 else 0.0
            P.op("vector", "scalar_tensor_tensor", xv, wo_bank(g, i), sc, xv,
                 ALU.add, ALU.add, inc="sV")
            wo_evac[(g, i)] = P.cnt("sV")

        emit_qdma(0)
        emit_qdma(1)
        emit_qstage(0)
        emit_qstage(1)

        scratch_free = [("sV", 0)]

        def emit_sum_block(h, th):
            if dbg and th == 0 and (h - 1) in rcp_dump:
                P.wait("vector", "sDB", rcp_dump[h - 1])
            P.wait("tensor", *scratch_free[0])
            P.wait("tensor", "sA", exp_at[(h, 30 + th)])
            for TC in range(16):
                P.op("tensor", "matmul", PS2[:, 512:1024], ones128[:, :],
                     pt_full[:, TC * 1024 + th * 512: TC * 1024 + th * 512 + 512],
                     start=(TC == 0), stop=(TC == 15),
                     inc="sP" if TC == 15 else None)
            spc = P.cnt("sP")
            P.wait("vector", "sP", spc)
            P.op("vector", "tensor_copy", gs_sb[:, th * 512:th * 512 + 512],
                 PS2[:, 512:1024], inc="sV")
            M[f"sumev_{h}_{th}"] = P.cnt("sV")
            scratch_free[0] = ("sV", M[f"sumev_{h}_{th}"])

        wo_queue = []

        def emit_wo_group_mm2(g, i):
            dc, hf = i // 2, i % 2
            P.wait("tensor", "sWO", wo_dma[(g, 3)])
            P.wait("tensor", *scratch_free[0])
            if i == 0:
                P.wait("tensor", "sV", ohev[(4 * g + 3, 1)])
                if g == 3:
                    P.wait("tensor", "sV", qevac[NH - 1])
                    P.wait("tensor", "sA", sqrt_done[NH - 1])
                    P.wait("tensor", "sV", negrow_done[NH - 1])
            for k in range(4):
                P.op("tensor", "matmul", PS2[:, 512:1024],
                     wog[:, k * 2048 + dc * 128: k * 2048 + (dc + 1) * 128],
                     oh[g % 2][:, k * T + hf * 512: k * T + hf * 512 + 512],
                     start=(k == 0), stop=(k == 3),
                     inc="sP" if k == 3 else None)
            wo_pe[(g, i)] = P.cnt("sP")
            P.wait("vector", "sP", wo_pe[(g, i)])
            xv = x_sb[:, dc * T + hf * 512: dc * T + hf * 512 + 512]
            sc = bol[:, dc:dc + 1] if g == 0 else 0.0
            P.op("vector", "scalar_tensor_tensor", xv, PS2[:, 512:1024], sc, xv,
                 ALU.add, ALU.add, inc="sV")
            wo_evac[(g, i)] = P.cnt("sV")
            scratch_free[0] = ("sV", wo_evac[(g, i)])

        def emit_av_mm(h, half):
            P.wait("tensor", "sA", exp_at[(h, 30 + half)])
            if half == 1:
                P.wait("tensor", "sV", ohev[(h, 0)])
            elif (h - 1, 1) in ohev:
                P.wait("tensor", "sV", ohev[(h - 1, 1)])
            if h == 0 and half == 0:
                P.wait("tensor", "sG", M["kv_gathered"])
            for TC in range(16):
                P.op("tensor", "matmul", PS2[:, 0:512],
                     vsb[:, TC * 128:(TC + 1) * 128],
                     pt_full[:, TC * 1024 + half * 512:
                             TC * 1024 + half * 512 + 512],
                     start=(TC == 0), stop=(TC == 15),
                     inc="sP" if TC == 15 else None)
            avpe[(h, half)] = P.cnt("sP")

        def emit_ohev(h, half):
            g = h // 4
            P.wait("vector", "sP", avpe[(h, half)])
            P.wait("vector", *recip_done[h])
            if h % 4 == 0 and half == 0 and g >= 2:
                P.wait("vector", "sP", wo_pe[(g - 2, 31)])
            P.op("vector", "tensor_tensor",
                 oh[g % 2][:, (h % 4) * T + half * 512:
                           (h % 4) * T + half * 512 + 512],
                 PS2[:, 0:512], gs_sb[:, half * 512:half * 512 + 512],
                 ALU.mult, inc="sV")
            ohev[(h, half)] = P.cnt("sV")

        for h in range(NH):
            hp = h % 2
            u = 0
            for TC in range(16):
                for th in range(2):
                    bank = PSB[:, (u % 4) * 512:(u % 4) * 512 + 512]
                    ug = 32 * h + u
                    if u <= 3:
                        P.wait("tensor", "sV", negrow_done[h])
                        if h == 0:
                            P.wait("tensor", "sG", M["kv_gathered"])
                            P.wait("tensor", "sV", M["maxk2"])
                    if ug >= 4:
                        P.wait("tensor", "sA", exp_g[ug - 4])
                    P.op("tensor", "matmul", bank,
                         ksb[:, TC * 128:(TC + 1) * 128],
                         qsb[hp][:, th * 512:th * 512 + 512],
                         start=True, stop=True, inc="sP")
                    sc_pe = P.cnt("sP")
                    if u == 31:
                        M[f"score_last_{h}"] = sc_pe
                    P.wait("scalar", "sP", sc_pe)
                    if (h - 1, th) in pt_free:
                        P.wait("scalar", "sP", pt_free[(h - 1, th)])
                    P.op("scalar", "activation",
                         pt_full[:, TC * 1024 + th * 512: TC * 1024 + th * 512 + 512],
                         bank, AF.Exp, scol[:, 4 + hp:5 + hp], 1.0, inc="sA")
                    exp_at[(h, u)] = exp_g[ug] = P.cnt("sA")
                    u += 1
                    if u == 16 and h + 2 < NH:
                        emit_qdma(h + 2)
                        emit_qmm(h + 2)
                    if u == 31:
                        emit_av_mm(h, 0)
                        emit_sum_block(h, 0)
                        pt_free[(h, 0)] = P.cnt("sP")
                    if u % 4 == 3 and wo_queue:
                        emit_wo_group_mm2(*wo_queue.pop(0))
            emit_sum_block(h, 1)
            if dbg:
                P.wait("gpsimd", "sV", M[f"sumev_{h}_1"])
                P.dma("gpsimd", dbg_sum[h:h + 1, :], gs_sb[0:1, :], "sDB")
                if h == 5:
                    P.dma("gpsimd", dbg_gsf[:, :], gs_sb[:, :], "sDB")
                    P.dma("gpsimd", dbg_ones[:, :], ones128[:, :], "sDB")
                sum_dump[h] = P.cnt("sDB")
                P.wait("scalar", "sDB", sum_dump[h])
            P.wait("scalar", "sV", M[f"sumev_{h}_1"])
            P.op("scalar", "activation", recip_sb[:, :], gs_sb[:, :], AF.Ln,
                 eps2_c[:, 0:1], 1.0, inc="sA")
            P.wait("scalar", "sA", P.cnt("sA"))
            P.op("scalar", "activation", gs_sb[:, :], recip_sb[:, :], AF.Exp,
                 0.0, -1.0, inc="sA")
            recip_done[h] = ("sA", P.cnt("sA"))
            if dbg:
                P.wait("gpsimd", "sA", recip_done[h][1])
                P.dma("gpsimd", dbg_rcp[h:h + 1, :], gs_sb[0:1, :], "sDB")
                if h == 5:
                    P.dma("gpsimd", dbg_rcf[:, :], gs_sb[:, :], "sDB")
                rcp_dump[h] = P.cnt("sDB")
            if h + 2 < NH:
                emit_qfinish(h + 2)
                emit_qbound(h + 2)
            emit_ohev(h, 0)
            emit_av_mm(h, 1)
            pt_free[(h, 1)] = P.cnt("sP")
            emit_ohev(h, 1)
            if h % 4 == 3:
                g = h // 4
                for k in range(4):
                    hh = 4 * g + k
                    P.wait("gpsimd", "sP",
                           M["ffn0_pe_done"] if g == 0 else wo_pe[(g - 1, 31)])
                    P.dma("gpsimd", wog[:, k * 2048:(k + 1) * 2048],
                          wo_d[hh * 128:(hh + 1) * 128, :], "sWO")
                    wo_dma[(g, k)] = P.cnt("sWO")
                wo_queue.extend((g, i) for i in range(32))
                if g == 3:
                    while wo_queue:
                        emit_wo_group_mm2(*wo_queue.pop(0))
        M["wo_pe_done"] = P.cnt("sP")
        M["attn_x_done"] = wo_evac[(3, 31)]
        M["wo_dma_done"] = wo_dma[(3, 3)]
        M["exp_last"] = exp_at[(15, 31)]
        M["ohev_last"] = ohev[(15, 1)]

    layernorm("ln1", xb16_a)
    ffn(0, "ln1")
    if phase >= 2:
        attention()
    if phase >= 3:
        if dbg:
            P.wait("gpsimd", "sV", M["attn_x_done"])
            for c in range(DCN):
                P.dma("gpsimd", dbg_x2[c * 128:(c + 1) * 128, :],
                      x_sb[:, c * T:(c + 1) * T], "sDB")
            M["x2_dump"] = P.cnt("sDB")
        layernorm("ln3", xb16_a,
                  pre_dve=(("sWO", M["wo_dma_done"]), ("sP", M["wo_pe_done"]),
                           ("sV", M["attn_x_done"])),
                  pre_pe=(("sA", M["exp_last"]), ("sV", M["attn_x_done"])))
        if dbg:
            M["dbg_gate"] = ("sDB", P.cnt("sDB"))
        ffn(1, "ln3")
        xfin = M["ffn1_x_done"]
    elif phase == 2:
        xfin = M["attn_x_done"]
    else:
        xfin = M["ffn0_x_done"]

    P.wait("sync", "sV", xfin)
    for c in range(DCN):
        P.dma("sync", y_out[c * 128:(c + 1) * 128, :], x_sb[:, c * T:(c + 1) * T])
    P.wait("sync", "sY", P.cnt("sY"))
    if dbg:
        P.wait("sync", "sDB", P.cnt("sDB"))

    with nc.Block() as block:
        block.sync(lambda e: [f(e) for f in P.ops["sync"]])
        block.tensor(lambda e: [f(e) for f in P.ops["tensor"]])
        block.vector(lambda e: [f(e) for f in P.ops["vector"]])
        block.scalar(lambda e: [f(e) for f in P.ops["scalar"]])
        block.gpsimd(lambda e: [f(e) for f in P.ops["gpsimd"]])
    ctx.close()
    return nc


def _host_layout(inputs):
    bf = ml_dtypes.bfloat16
    f32 = np.float32
    g = lambda k: np.asarray(inputs[k])
    Wq, Wk, Wv, Wo = (np.ascontiguousarray(np.asarray(g(k), f32)) for k in
                      ("L0_attn_Wq", "L0_attn_Wk", "L0_attn_Wv", "L0_attn_Wo"))
    shared = {
        "wq": np.ascontiguousarray(
            Wq.reshape(16, 128, 16, 128).transpose(2, 1, 0, 3).reshape(D, D)),
        "wk": np.ascontiguousarray(
            Wk.reshape(16, 128, 128).transpose(1, 0, 2).reshape(128, D)),
        "wv": np.ascontiguousarray(
            Wv.reshape(16, 128, 128).transpose(1, 0, 2).reshape(128, D)),
        "wo": Wo,
        "gw": np.ascontiguousarray(
            np.asarray(g("L0_ffn_gate_w"), f32).reshape(ICN, 128, 16, 128)
            .transpose(0, 3, 2, 1).reshape(ICN * 128, D)).astype(bf),
        "uw": np.ascontiguousarray(
            np.asarray(g("L0_ffn_up_w"), f32).reshape(ICN, 128, 16, 128)
            .transpose(0, 3, 2, 1).reshape(ICN * 128, D)).astype(bf),
        "dw": np.ascontiguousarray(
            np.asarray(g("L0_ffn_down_w"), f32).T.reshape(ICN, 128, 2, 1024)
            .transpose(2, 0, 1, 3).reshape(2 * ICN * 128, 1024)).astype(bf),
        "bq_l": np.ascontiguousarray(np.asarray(g("L0_attn_bq"), f32).reshape(16, 128).T),
        "bk_c": np.ascontiguousarray(np.asarray(g("L0_attn_bk"), f32).reshape(128, 1)),
        "bv_r": np.ascontiguousarray(np.asarray(g("L0_attn_bv"), f32).reshape(1, 128)),
        "bo_l": np.ascontiguousarray(np.asarray(g("L0_attn_bo"), f32).reshape(16, 128).T),
        "gs_l": np.ascontiguousarray(np.asarray(g("L0_ffn_gate_s"), f32).reshape(ICN, 128).T),
        "us_l": np.ascontiguousarray(np.asarray(g("L0_ffn_up_s"), f32).reshape(ICN, 128).T),
        "ds_l": np.ascontiguousarray(np.asarray(g("L0_ffn_down_s"), f32).reshape(16, 128).T),
    }
    x = np.asarray(inputs["L0_attn_x"], f32)
    in_maps = []
    for c in range(NCORES):
        b, hf = c // 2, c % 2
        m = dict(shared)
        m["x_in"] = np.ascontiguousarray(x[b, hf * T:(hf + 1) * T, :].T)
        in_maps.append(m)
    return in_maps


def _run(inputs, phase=3):
    in_maps = _host_layout(inputs)
    nc = build(phase)
    res = run_bass_kernel_spmd(nc, in_maps, core_ids=list(range(NCORES)))
    y = np.empty((4, 2048, D), np.float32)
    for c, r in enumerate(res.results):
        y[c // 2, (c % 2) * T:(c % 2 + 1) * T, :] = r["y_out"].T
    return y


def kernel(**inputs):
    return _run(inputs, phase=3)


# version 16
# speedup vs baseline: 1.1380x; 1.1380x over previous
import math
from contextlib import ExitStack

import numpy as np
import ml_dtypes

import concourse.bass as bass
import concourse.mybir as mybir
from concourse.bass_utils import run_bass_kernel_spmd

F32 = mybir.dt.float32
BF16 = mybir.dt.bfloat16
AF = mybir.ActivationFunctionType
ALU = mybir.AluOpType
AX = mybir.AxisListType

NCORES = 8
D = 2048
T = 1024
NH = 16
INTER = 5632
ICN = INTER // 128
DCN = D // 128
EPS = 1e-5
RSQ = 1.0 / math.sqrt(128.0)
QUAD_ORDER = ((0, 0), (0, 1), (1, 0), (1, 1))

ENGINES = ("sync", "scalar", "vector", "tensor", "gpsimd")


class Plan:

    def __init__(self):
        self.ops = {e: [] for e in ENGINES}
        self.sems = {}
        self.marks = {}

    def add_sem(self, name, handle):
        self.sems[name] = [handle, 0]

    def cnt(self, name):
        return self.sems[name][1]

    def emit(self, eng, fn):
        self.ops[eng].append(fn)

    def wait(self, eng, sname, val):
        if val is None or val <= 0:
            return
        h, n = self.sems[sname]
        assert val <= n, f"wait {sname}>={val} but only {n} planned (eng={eng})"
        key = (eng, sname)
        if self.marks.get(key, 0) >= val:
            return
        self.marks[key] = val
        self.emit(eng, lambda e, h=h, v=val: e.wait_ge(h, v))

    def _inc(self, sname, k):
        rec = self.sems[sname]
        rec[1] += k
        return rec[0], k

    def op(self, eng, meth, *args, inc=None, inck=1, **kw):
        tup = self._inc(inc, inck) if inc else None

        def f(e, meth=meth, args=args, kw=kw, tup=tup):
            ins = getattr(e, meth)(*args, **kw)
            if tup is not None:
                ins.then_inc(tup[0], tup[1])

        self.emit(eng, f)
        return self.sems[inc][1] if inc else None

    def dma(self, eng, dst, src, inc="sY", **kw):
        return self.op(eng, "dma_start", dst, src, inc=inc, inck=16, **kw)


def build(phase=3, dbg=False):
    nc = bass.Bass("TRN2", target_bir_lowering=False)

    x_in = nc.dram_tensor("x_in", [D, T], F32, kind="ExternalInput")
    wq_d = nc.dram_tensor("wq", [D, D], F32, kind="ExternalInput")
    wk_d = nc.dram_tensor("wk", [128, D], F32, kind="ExternalInput")
    wv_d = nc.dram_tensor("wv", [128, D], F32, kind="ExternalInput")
    wo_d = nc.dram_tensor("wo", [D, D], F32, kind="ExternalInput")
    gw_d = nc.dram_tensor("gw", [ICN * 128, D], BF16, kind="ExternalInput")
    uw_d = nc.dram_tensor("uw", [ICN * 128, D], BF16, kind="ExternalInput")
    dw_d = nc.dram_tensor("dw", [2 * ICN * 128, T], BF16, kind="ExternalInput")
    bq_d = nc.dram_tensor("bq_l", [128, 16], F32, kind="ExternalInput")
    bk_d = nc.dram_tensor("bk_c", [128, 1], F32, kind="ExternalInput")
    bv_d = nc.dram_tensor("bv_r", [1, 128], F32, kind="ExternalInput")
    bo_d = nc.dram_tensor("bo_l", [128, 16], F32, kind="ExternalInput")
    gs_d = nc.dram_tensor("gs_l", [128, ICN], F32, kind="ExternalInput")
    us_d = nc.dram_tensor("us_l", [128, ICN], F32, kind="ExternalInput")
    ds_d = nc.dram_tensor("ds_l", [128, 16], F32, kind="ExternalInput")
    y_out = nc.dram_tensor("y_out", [D, T], F32, kind="ExternalOutput")
    if dbg:
        dbg_ln = [nc.dram_tensor(f"dbg_ln{i}", [D, T], F32, kind="ExternalOutput")
                  for i in (1, 2, 3)]
        dbg_x2 = nc.dram_tensor("dbg_x2", [D, T], F32, kind="ExternalOutput")
        dbg_bnd = nc.dram_tensor("dbg_bnd", [16, 4], F32, kind="ExternalOutput")
        dbg_sum = nc.dram_tensor("dbg_sum", [16, T], F32, kind="ExternalOutput")
        dbg_rcp = nc.dram_tensor("dbg_rcp", [16, T], F32, kind="ExternalOutput")
        dbg_gsf = nc.dram_tensor("dbg_gsf", [128, T], F32, kind="ExternalOutput")
        dbg_rcf = nc.dram_tensor("dbg_rcf", [128, T], F32, kind="ExternalOutput")
        dbg_ones = nc.dram_tensor("dbg_ones", [128, 128], BF16,
                                  kind="ExternalOutput")
        dbg_h = nc.dram_tensor("dbg_h", [ICN * 128, T], F32, kind="ExternalOutput")
        dbg_ab = nc.dram_tensor("dbg_ab", [6, T], F32, kind="ExternalOutput")
    if phase >= 2:
        kv_in = nc.dram_tensor("kv_in", [256, T], BF16, kind="Internal")
        kv_gat = nc.dram_tensor("kv_gat", [512, T], BF16, kind="Internal")

    BASE = 16512
    A = lambda name, shape, dtype, offset: nc.alloc_sbuf_tensor_at(
        name, shape, dtype, offset=BASE + offset)
    x_sb = A("x_sb", [128, DCN * T], F32, offset=0)
    LNO = 65536
    ln_sb = A("ln_sb", [128, DCN * T], BF16, offset=LNO)
    HO = 98304
    h_sb = A("h_sb", [128, ICN * T], BF16, offset=HO)
    wog = A("wog", [128, 4 * 2048], BF16, offset=HO + 0)
    wqb = [A(f"wqb{i}", [128, 2048], BF16, offset=HO + 16384 + i * 4096) for i in range(2)]
    qsb = [A(f"qsb{i}", [128, T], BF16, offset=HO + 24576 + i * 2048) for i in range(2)]
    pt_full = A("pt_full", [128, 16 * 1024], BF16, offset=HO + 28672)
    xb16_b = A("xb16_b", [128, DCN * T], BF16, offset=HO + 28672)
    oh = [A(f"oh{i}", [128, 4 * T], BF16, offset=HO + 61440 + i * 8192) for i in range(2)]
    rowA = A("rowA", [1, T], F32, offset=HO + 77824)
    rowB = A("rowB", [1, T], F32, offset=HO + 81920)
    rowT = A("rowT", [1, T], F32, offset=HO + 86016)
    xb16_a = A("xb16_a", [128, DCN * T], BF16, offset=HO + 0)
    R1 = 188416
    gsl = [A(f"gsl{i}", [128, 2048], BF16, offset=R1 + i * 4096) for i in range(2)]
    usl = [A(f"usl{i}", [128, 2048], BF16, offset=R1 + 8192 + i * 4096) for i in range(2)]
    gs_sb = A("gs_sb", [128, T], F32, offset=R1 + 16384)
    dsl = [A(f"dsl{j}", [128, T], BF16, offset=R1 + j * 4096) for j in range(4)]
    ksb = A("ksb", [128, 2048], BF16, offset=R1 + 0)
    vsb = A("vsb", [128, 2048], BF16, offset=R1 + 4096)
    wkb = A("wkb", [128, 2048], BF16, offset=R1 + 8192)
    wvb = A("wvb", [128, 2048], BF16, offset=R1 + 12288)
    recip_sb = A("recip_sb", [128, 1024], F32, offset=R1 + 8192)
    ksq = A("ksq", [128, 2048], BF16, offset=R1 + 12288)
    qsq = A("qsq", [128, 1024], BF16, offset=R1 + 12288)
    negrow = [A(f"negrow{i}", [1, 1024], BF16, offset=HO + 86016 + i * 2048)
              for i in range(2)]
    SM = 208896
    bqs = A("bqs", [128, 16], F32, offset=SM)
    bkc = A("bkc", [128, 1], F32, offset=SM + 64)
    bol = A("bol", [128, 16], F32, offset=SM + 96)
    gss = A("gss", [128, ICN], F32, offset=SM + 160)
    uss = A("uss", [128, ICN], F32, offset=SM + 352)
    dsh = A("dsh", [128, 16], F32, offset=SM + 544)
    bql_raw = A("bql_raw", [128, 16], F32, offset=SM + 608)
    dsl_raw = A("dsl_raw", [128, 16], F32, offset=SM + 672)
    scol = A("scol", [128, 8], F32, offset=SM + 736)
    ones_cb = A("ones_cb", [128, 1], BF16, offset=SM + 768)
    ones128 = A("ones128", [128, 128], BF16, offset=SM + 2368)
    onesf_r = A("onesf_r", [1, 128], F32, offset=SM + 800)
    onesb_r = A("onesb_r", [1, 128], BF16, offset=SM + 1312)
    bvr_f = A("bvr_f", [1, 128], F32, offset=SM + 1568)
    bvr_b = A("bvr_b", [1, 128], BF16, offset=SM + 2080)
    eps_c = A("eps_c", [1, 1], F32, offset=SM + 2336)
    eps2_c = A("eps2_c", [128, 1], F32, offset=SM + 2624)

    PSB = nc.alloc_psum_tensor("PSB", [128, 2048], F32)
    PS2 = nc.alloc_psum_tensor("PS2", [128, 1024], F32)
    PS3 = nc.alloc_psum_tensor("PS3", [128, 1024], F32)

    P = Plan()
    ctx = ExitStack()
    for s in ("sY", "sG", "sC", "sP", "sV", "sA", "sM",
              "sS0", "sS1", "sS2", "sS3", "sH0", "sH1", "sWO", "sKA", "sKB",
              "sPR0", "sPR1", "sPR2", "sDB"):
        P.add_sem(s, ctx.enter_context(nc.semaphore(s)))
    M = {}

    P.op("gpsimd", "memset", ones_cb[:, :], 1.0)
    P.op("gpsimd", "memset", onesf_r[:, :], 1.0)
    P.op("gpsimd", "memset", onesb_r[:, :], 1.0)
    P.op("gpsimd", "memset", ones128[:, :], 1.0)
    P.op("gpsimd", "memset", eps_c[:, :], EPS)
    P.op("gpsimd", "memset", eps2_c[:, :], 1e-30, inc="sM")
    for dst, src in ((bql_raw, bq_d), (bkc, bk_d), (bvr_f, bv_d), (bol, bo_d),
                     (gss, gs_d), (uss, us_d), (dsl_raw, ds_d)):
        P.dma("gpsimd", dst[:, :], src[:, :], "sG")
    M["setup_dmas"] = P.cnt("sG")
    P.wait("gpsimd", "sG", M["setup_dmas"])
    P.op("gpsimd", "tensor_copy", bvr_b[:, :], bvr_f[:, :], inc="sM")
    M["setup_gp"] = P.cnt("sM")
    P.wait("vector", "sG", M["setup_dmas"])
    P.op("vector", "tensor_scalar", bqs[:, :], bql_raw[:, :], RSQ, None, ALU.mult)
    P.op("vector", "tensor_scalar", dsh[:, :], dsl_raw[:, :], 0.5, None, ALU.mult,
         inc="sV")
    M["setup_v"] = P.cnt("sV")
    P.wait("vector", "sV", M["setup_v"])

    for c in range(DCN):
        P.dma("sync", x_sb[:, c * T:(c + 1) * T], x_in[c * 128:(c + 1) * 128, :])
    M["x_loaded"] = P.cnt("sY")

    def layernorm(tag, xb16, pre_dve=(), pre_pe=()):
        for sn, v in pre_dve:
            P.wait("scalar", sn, v)
        for sn, v in pre_pe:
            P.wait("scalar", sn, v)
        if tag == "ln1":
            P.wait("scalar", "sY", M["x_loaded"])
        P.wait("scalar", "sV", P.cnt("sV"))
        for sn, v in pre_dve:
            P.wait("vector", sn, v)
        if dbg and tag != "ln1":
            P.wait("vector", "sDB", P.cnt("sDB"))
        act_done, dve_done = [], []
        for c in range(DCN):
            P.op("scalar", "activation", xb16[:, c * T:(c + 1) * T],
                 x_sb[:, c * T:(c + 1) * T], AF.Copy, 0.0, 1.0, inc="sA")
            act_done.append(P.cnt("sA"))
        for c in range(DCN):
            P.wait("vector", "sA", act_done[c])
            P.op("vector", "tensor_tensor", ln_sb[:, c * T:(c + 1) * T],
                 xb16[:, c * T:(c + 1) * T], xb16[:, c * T:(c + 1) * T], ALU.mult,
                 inc="sV")
            dve_done.append(P.cnt("sV"))
        for sn, v in pre_pe:
            P.wait("tensor", sn, v)
        P.wait("tensor", "sM", M["setup_gp"])
        for c in range(DCN):
            P.wait("tensor", "sA", act_done[c])
            P.wait("tensor", "sV", dve_done[c])
            st, sp = (c == 0), (c == DCN - 1)
            for hf in range(2):
                s0 = c * T + hf * 512
                P.op("tensor", "matmul", PSB[:, hf * 512:hf * 512 + 512],
                     ones128[:, :], xb16[:, s0:s0 + 512], start=st, stop=sp)
                P.op("tensor", "matmul", PSB[:, 1024 + hf * 512:1536 + hf * 512],
                     ones128[:, :], ln_sb[:, s0:s0 + 512], start=st, stop=sp,
                     inc="sP" if (sp and hf == 1) else None)
        M[tag + "_stats"] = P.cnt("sP")
        P.wait("vector", "sP", M[tag + "_stats"])
        P.op("vector", "tensor_scalar", rowB[:, :], PSB[0:1, 0:1024], 1.0 / D,
             None, ALU.mult, inc="sV")
        P.op("vector", "tensor_scalar", rowA[:, :], PSB[0:1, 1024:2048], 1.0 / D,
             None, ALU.mult, inc="sV")
        P.wait("vector", "sV", P.cnt("sV"))
        P.op("vector", "tensor_tensor", rowT[:, :], rowB[:, :], rowB[:, :],
             ALU.mult, inc="sV")
        P.wait("vector", "sV", P.cnt("sV"))
        P.op("vector", "tensor_tensor", rowA[:, :], rowA[:, :], rowT[:, :],
             ALU.subtract, inc="sV")
        M[tag + "_var"] = P.cnt("sV")
        P.wait("scalar", "sV", M[tag + "_var"])
        P.wait("scalar", "sM", M["setup_gp"])
        P.op("scalar", "activation", rowT[:, :], rowA[:, :], AF.Ln,
             eps_c[0:1, 0:1], 1.0, inc="sA")
        P.wait("scalar", "sA", P.cnt("sA"))
        P.op("scalar", "activation", rowA[:, :], rowT[:, :], AF.Exp,
             0.0, -0.5, inc="sA")
        M[tag + "_sqrt"] = P.cnt("sA")
        P.wait("vector", "sA", M[tag + "_sqrt"])
        P.op("vector", "tensor_tensor", rowB[:, :], rowB[:, :], rowA[:, :],
             ALU.mult, inc="sV")
        M[tag + "_ab"] = P.cnt("sV")
        P.wait("tensor", "sV", M[tag + "_ab"])
        for hf in range(2):
            sl = slice(hf * 512, hf * 512 + 512)
            P.op("tensor", "matmul", PS2[:, sl], onesf_r[:, :], rowA[0:1, sl],
                 start=True, stop=True)
            P.op("tensor", "matmul", PS3[:, sl], onesf_r[:, :], rowB[0:1, sl],
                 start=True, stop=True, inc="sP" if hf == 1 else None)
        M[tag + "_bcast"] = P.cnt("sP")
        P.wait("vector", "sP", M[tag + "_bcast"])
        for c in range(DCN):
            P.wait("vector", "sV", P.cnt("sV"))
            P.op("vector", "tensor_tensor", gs_sb[:, :], x_sb[:, c * T:(c + 1) * T],
                 PS2[:, :], ALU.mult, inc="sV")
            P.wait("vector", "sV", P.cnt("sV"))
            P.op("vector", "tensor_tensor", ln_sb[:, c * T:(c + 1) * T],
                 gs_sb[:, :], PS3[:, :], ALU.subtract, inc="sV")
            M.setdefault(tag + "_applyc", []).append(P.cnt("sV"))
        M[tag + "_apply"] = P.cnt("sV")
        if dbg:
            i = {"ln1": 0, "ln2": 1, "ln3": 2}[tag]
            P.wait("gpsimd", "sV", M[tag + "_apply"])
            for c in range(DCN):
                P.dma("gpsimd", dbg_ln[i][c * 128:(c + 1) * 128, :],
                      ln_sb[:, c * T:(c + 1) * T], "sDB")
            P.dma("gpsimd", dbg_ab[2 * i:2 * i + 1, :], rowA[0:1, :], "sDB")
            P.dma("gpsimd", dbg_ab[2 * i + 1:2 * i + 2, :], rowB[0:1, :], "sDB")
            M[tag + "_dump"] = P.cnt("sDB")

    def ffn(f, ln_tag):
        g_ps = (PSB[:, 0:1024], PS2[:, :])
        u_ps = (PSB[:, 1024:2048], PS3[:, :])
        slab_loaded, g_done, u_done, silu_done, hmul_done = {}, {}, {}, {}, {}

        def emit_slab(ic):
            sg, su = f"sS{ic % 2}", f"sS{2 + ic % 2}"
            if ic >= 2:
                P.wait("sync", "sP", g_done[ic - 2])
            elif f == 1:
                P.wait("sync", "sP", M["wo_pe_done"])
                for j in range(3):
                    P.wait("sync", f"sPR{j}", P.cnt(f"sPR{j}"))
            P.dma("sync", gsl[ic % 2][:, :], gw_d[ic * 128:(ic + 1) * 128, :], sg)
            if ic >= 2:
                P.wait("sync", "sP", u_done[ic - 2])
            P.dma("sync", usl[ic % 2][:, :], uw_d[ic * 128:(ic + 1) * 128, :], su)
            slab_loaded[ic] = ((sg, P.cnt(sg)), (su, P.cnt(su)))

        emit_slab(0)
        emit_slab(1)
        for ic in range(ICN):
            gp, up = g_ps[ic % 2], u_ps[ic % 2]
            for sn, v in slab_loaded[ic]:
                P.wait("tensor", sn, v)
            if ic == 1:
                P.wait("tensor", "sV", M[ln_tag + "_apply"])
            if ic >= 2:
                P.wait("tensor", "sA", silu_done[ic - 2])
            for hc in range(DCN):
                if ic == 0:
                    P.wait("tensor", "sV", M[ln_tag + "_applyc"][hc])
                for hf in range(2):
                    s0 = hc * T + hf * 512
                    P.op("tensor", "matmul", gp[:, hf * 512:hf * 512 + 512],
                         gsl[ic % 2][:, hc * 128:(hc + 1) * 128],
                         ln_sb[:, s0:s0 + 512], start=(hc == 0),
                         stop=(hc == DCN - 1),
                         inc="sP" if (hc == DCN - 1 and hf == 1) else None)
            g_done[ic] = P.cnt("sP")
            if ic >= 2:
                P.wait("tensor", "sV", hmul_done[ic - 2])
            for hc in range(DCN):
                for hf in range(2):
                    s0 = hc * T + hf * 512
                    P.op("tensor", "matmul", up[:, hf * 512:hf * 512 + 512],
                         usl[ic % 2][:, hc * 128:(hc + 1) * 128],
                         ln_sb[:, s0:s0 + 512], start=(hc == 0),
                         stop=(hc == DCN - 1),
                         inc="sP" if (hc == DCN - 1 and hf == 1) else None)
            u_done[ic] = P.cnt("sP")
            P.wait("scalar", "sP", g_done[ic])
            if ic >= 1:
                P.wait("scalar", "sV", hmul_done[ic - 1])
            P.op("scalar", "activation", gs_sb[:, :], gp[:, :], AF.Silu, 0.0,
                 gss[:, ic:ic + 1], inc="sA")
            silu_done[ic] = P.cnt("sA")
            P.wait("vector", "sP", u_done[ic])
            P.wait("vector", "sA", silu_done[ic])
            if ic == 0 and "dbg_gate" in M:
                P.wait("vector", *M["dbg_gate"])
            P.op("vector", "scalar_tensor_tensor", h_sb[:, ic * T:(ic + 1) * T],
                 up[:, :], uss[:, ic:ic + 1], gs_sb[:, :], ALU.mult, ALU.mult,
                 inc="sV")
            hmul_done[ic] = P.cnt("sV")
            if ic + 2 < ICN:
                emit_slab(ic + 2)

        def bank(dc8):
            if dc8 < 4:
                return PSB[:, dc8 * 512:dc8 * 512 + 512]
            t = PS2 if dc8 < 6 else PS3
            return t[:, (dc8 % 2) * 512:(dc8 % 2) * 512 + 512]

        dwn_loaded, ic_pe = {}, {}
        epi = {qq: {} for qq in range(4)}
        for qq, (dh, th) in enumerate(QUAD_ORDER):
            def emit_dslab(ic, qq=qq, dh=dh):
                slot = ic % 4
                if qq == 0 and ic < 4:
                    P.wait("sync", "sP",
                           (g_done if slot < 2 else u_done)[ICN - 2 + slot % 2])
                elif ic < 4:
                    P.wait("sync", "sP", ic_pe[(qq - 1, ICN - 4 + slot)])
                else:
                    P.wait("sync", "sP", ic_pe[(qq, ic - 4)])
                r0 = (dh * ICN + ic) * 128
                P.dma("sync", dsl[slot][:, :], dw_d[r0:r0 + 128, :], f"sS{slot}")
                dwn_loaded[(qq, ic)] = (f"sS{slot}", P.cnt(f"sS{slot}"))

            for ic in range(4):
                emit_dslab(ic)
            for ic in range(ICN):
                P.wait("tensor", *dwn_loaded[(qq, ic)])
                if ic == 0 and qq == 0:
                    P.wait("tensor", "sV", hmul_done[ICN - 1])
                for dc8 in range(8):
                    if ic == 0 and qq > 0:
                        P.wait("tensor", "sV", epi[qq - 1][dc8])
                    P.op("tensor", "matmul", bank(dc8),
                         dsl[ic % 4][:, dc8 * 128:(dc8 + 1) * 128],
                         h_sb[:, ic * T + th * 512: ic * T + th * 512 + 512],
                         start=(ic == 0), stop=(ic == ICN - 1),
                         inc="sP" if dc8 == 7 else None)
                ic_pe[(qq, ic)] = P.cnt("sP")
                if ic + 4 < ICN:
                    emit_dslab(ic + 4)
            P.wait("vector", "sP", ic_pe[(qq, ICN - 1)])
            for dc8 in range(8):
                dc = dh * 8 + dc8
                xv = x_sb[:, dc * T + th * 512: dc * T + th * 512 + 512]
                P.op("vector", "scalar_tensor_tensor", xv, bank(dc8),
                     dsh[:, dc:dc + 1], xv, ALU.mult, ALU.add, inc="sV")
                epi[qq][dc8] = P.cnt("sV")
        M[f"ffn{f}_pe_done"] = P.cnt("sP")
        M[f"ffn{f}_x_done"] = P.cnt("sV")
        if dbg and f == 1:
            P.wait("gpsimd", "sV", hmul_done[ICN - 1])
            for ic in range(ICN):
                P.dma("gpsimd", dbg_h[ic * 128:(ic + 1) * 128, :],
                      h_sb[:, ic * T:(ic + 1) * T], "sDB")

    def attention():
        layernorm("ln2", xb16_b,
                  pre_dve=(("sP", M["ffn0_pe_done"]),),
                  pre_pe=(("sV", M["ffn0_x_done"]),))
        P.wait("gpsimd", "sP", M["ffn0_pe_done"])
        P.dma("gpsimd", wkb[:, :], wk_d[:, :], "sKA")
        P.dma("gpsimd", wvb[:, :], wv_d[:, :], "sKB")
        M["wkv_dma"] = (P.cnt("sKA"), P.cnt("sKB"))
        P.wait("tensor", "sKA", M["wkv_dma"][0])
        P.wait("tensor", "sKB", M["wkv_dma"][1])
        for hc in range(DCN):
            P.wait("tensor", "sV", M["ln2_applyc"][hc])
            for hf in range(2):
                P.op("tensor", "matmul", PSB[:, hf * 512:hf * 512 + 512],
                     wkb[:, hc * 128:(hc + 1) * 128],
                     ln_sb[:, hc * T + hf * 512: hc * T + hf * 512 + 512],
                     start=(hc == 0), stop=(hc == DCN - 1),
                     inc="sP" if (hc == DCN - 1 and hf == 1) else None)
        M["k_pe"] = P.cnt("sP")
        for tc in range(8):
            for hc in range(DCN):
                P.op("tensor", "matmul", PSB[:, 1024 + tc * 128:1152 + tc * 128],
                     ln_sb[:, hc * T + tc * 128: hc * T + tc * 128 + 128],
                     wvb[:, hc * 128:(hc + 1) * 128], start=(hc == 0), stop=False)
            P.op("tensor", "matmul", PSB[:, 1024 + tc * 128:1152 + tc * 128],
                 onesb_r[:, :], bvr_b[:, :], start=False, stop=True,
                 inc="sP" if tc == 7 else None)
        M["v_pe"] = P.cnt("sP")
        P.wait("vector", "sP", M["k_pe"])
        P.op("vector", "tensor_scalar", ksb[:, 0:1024], PSB[:, 0:1024], bkc[:, 0:1],
             None, ALU.add, inc="sV")
        P.wait("vector", "sP", M["v_pe"])
        P.op("vector", "tensor_copy", vsb[:, 0:1024], PSB[:, 1024:2048], inc="sV")
        M["kv_evac"] = P.cnt("sV")
        P.wait("gpsimd", "sV", M["kv_evac"])
        P.dma("gpsimd", kv_in[0:128, :], ksb[:, 0:1024], "sG")
        P.dma("gpsimd", kv_in[128:256, :], vsb[:, 0:1024], "sG")
        P.wait("gpsimd", "sG", P.cnt("sG"))
        P.op("gpsimd", "collective_compute", "AllGather", ALU.bypass,
             [[0, 1], [2, 3], [4, 5], [6, 7]],
             [kv_in.ap().opt()], [kv_gat.ap().opt()], inc="sC")
        P.wait("gpsimd", "sC", P.cnt("sC"))
        for r in range(2):
            P.dma("gpsimd", ksb[:, r * 1024:(r + 1) * 1024],
                  kv_gat[r * 256:r * 256 + 128, :], "sG")
            P.dma("gpsimd", vsb[:, r * 1024:(r + 1) * 1024],
                  kv_gat[r * 256 + 128:r * 256 + 256, :], "sG")
        M["kv_gathered"] = P.cnt("sG")

        P.wait("scalar", "sG", M["kv_gathered"])
        P.wait("scalar", "sP", M["v_pe"])
        P.op("scalar", "activation", ksq[:, :], ksb[:, :], AF.Square, 0.0, 1.0,
             inc="sA")
        M["ksq"] = P.cnt("sA")
        P.wait("tensor", "sA", M["ksq"])
        P.wait("tensor", "sV", M["kv_evac"])
        for kk in range(4):
            P.op("tensor", "matmul", PSB[:, kk * 512:(kk + 1) * 512],
                 ones128[:, :], ksq[:, kk * 512:(kk + 1) * 512],
                 start=True, stop=True, inc="sP" if kk == 3 else None)
        M["kn2_pe"] = P.cnt("sP")
        P.wait("vector", "sP", M["kn2_pe"])
        P.op("vector", "reduce_max", scol[0:1, 0:1], PSB[0:1, 0:2048], AX.X,
             inc="sV")
        P.wait("vector", "sV", P.cnt("sV"))
        M["maxk2v"] = P.cnt("sV")
        P.wait("scalar", "sV", M["maxk2v"])
        P.op("scalar", "activation", scol[0:1, 1:2], scol[0:1, 0:1], AF.Ln,
             0.0, 1.0, inc="sA")
        M["lnmaxk2"] = P.cnt("sA")
        P.wait("vector", "sA", M["lnmaxk2"])
        P.op("vector", "tensor_scalar", scol[0:1, 1:2], scol[0:1, 1:2], 0.5,
             None, ALU.mult, inc="sV")
        P.wait("vector", "sV", P.cnt("sV"))
        M["maxk2"] = P.cnt("sV")

        qdma, qpe, qevac, sqrt_done, negrow_done = {}, {}, {}, {-1: 0}, {}
        exp_at, exp_g, avpe, ohev, pt_free = {}, {}, {}, {}, {}
        wo_pe, wo_evac, wo_dma = {}, {}, {}
        recip_done = {}
        wo_queue = []
        bnd_dump, sum_dump, rcp_dump = {}, {}, {}

        def emit_qdma(hh):
            P.wait("gpsimd", "sP",
                   M["ffn0_pe_done"] if hh < 2 else qpe[hh - 2])
            P.dma("gpsimd", wqb[hh % 2][:, :], wq_d[hh * 128:(hh + 1) * 128, :],
                  f"sH{hh % 2}")
            qdma[hh] = (f"sH{hh % 2}", P.cnt(f"sH{hh % 2}"))

        def emit_qmm(hh):
            P.wait("tensor", *qdma[hh])
            P.wait("tensor", "sV", M["kv_evac"] if hh == 0 else qevac[hh - 1])
            P.wait("tensor", "sA", sqrt_done[hh - 1])
            if hh >= 1:
                P.wait("tensor", "sV", negrow_done[hh - 1])
            for hc in range(DCN):
                for hf in range(2):
                    P.op("tensor", "matmul", PS3[:, hf * 512:hf * 512 + 512],
                         wqb[hh % 2][:, hc * 128:(hc + 1) * 128],
                         ln_sb[:, hc * T + hf * 512: hc * T + hf * 512 + 512],
                         start=(hc == 0), stop=(hc == DCN - 1),
                         inc="sP" if (hc == DCN - 1 and hf == 1) else None)
            qpe[hh] = P.cnt("sP")

        def emit_qfinish(hh):
            P.wait("vector", "sP", qpe[hh])
            if hh >= 2:
                P.wait("vector", "sP", M[f"score_last_{hh - 2}"])
            P.op("vector", "tensor_scalar", qsb[hh % 2][:, :], PS3[:, :], RSQ,
                 bqs[:, hh:hh + 1], ALU.mult, ALU.add, inc="sV")
            qevac[hh] = P.cnt("sV")

        def emit_qbound(hh):
            P.wait("vector", "sV", qevac[hh])
            if hh >= 1:
                P.wait("vector", "sP", M[f"qn2_{hh - 1}"])
            else:
                P.wait("vector", "sP", M["kn2_pe"])
            P.op("vector", "tensor_tensor", qsq[:, :], qsb[hh % 2][:, :],
                 qsb[hh % 2][:, :], ALU.mult, inc="sV")
            sq_a = P.cnt("sV")
            P.wait("tensor", "sV", sq_a)
            for hf in range(2):
                P.op("tensor", "matmul", PS3[:, hf * 512:hf * 512 + 512],
                     ones128[:, :], qsq[:, hf * 512:hf * 512 + 512],
                     start=True, stop=True, inc="sP" if hf == 1 else None)
            M[f"qn2_{hh}"] = P.cnt("sP")
            P.wait("scalar", "sP", M[f"qn2_{hh}"])
            P.wait("scalar", "sV", M["maxk2"])
            if hh >= 1:
                P.wait("scalar", "sV", negrow_done[hh - 1])
            P.op("scalar", "activation", rowB[:, :], PS3[0:1, 0:1024], AF.Ln,
                 0.0, 1.0, inc="sA")
            lnq = P.cnt("sA")
            P.wait("vector", "sA", lnq)
            P.op("vector", "reduce_max", scol[0:1, 2:3], rowB[:, :], AX.X,
                 inc="sV")
            P.wait("vector", "sV", P.cnt("sV"))
            mx = P.cnt("sV")
            P.wait("scalar", "sV", mx)
            P.op("scalar", "activation", scol[0:1, 3:4], scol[0:1, 2:3], AF.Exp,
                 scol[0:1, 1:2], 0.5, inc="sA")
            bexp = P.cnt("sA")
            P.wait("vector", "sA", bexp)
            P.op("vector", "tensor_scalar", scol[0:1, 3:4], scol[0:1, 3:4],
                 -1.0, None, ALU.mult, inc="sV")
            P.wait("vector", "sV", P.cnt("sV"))
            nb = P.cnt("sV")
            P.wait("tensor", "sV", nb)
            P.op("tensor", "matmul", PS3[:, 0:1], onesf_r[:, :],
                 scol[0:1, 3:4], start=True, stop=True, inc="sP")
            nbp = P.cnt("sP")
            P.wait("vector", "sP", nbp)
            if (hh - 2, 31) in exp_at:
                P.wait("vector", "sA", exp_at[(hh - 2, 31)])
            if dbg and (hh - 2) in bnd_dump:
                P.wait("vector", "sDB", bnd_dump[hh - 2])
            P.op("vector", "tensor_copy", scol[:, 4 + hh % 2:5 + hh % 2],
                 PS3[:, 0:1], inc="sV")
            P.wait("vector", "sV", P.cnt("sV"))
            sqrt_done[hh] = bexp
            negrow_done[hh] = P.cnt("sV")
            if dbg:
                P.wait("gpsimd", "sV", negrow_done[hh])
                P.dma("gpsimd", dbg_bnd[hh:hh + 1, 0:1],
                      scol[0:1, 4 + hh % 2:5 + hh % 2], "sDB")
                P.dma("gpsimd", dbg_bnd[hh:hh + 1, 1:2], scol[0:1, 2:3], "sDB")
                P.dma("gpsimd", dbg_bnd[hh:hh + 1, 2:3], scol[0:1, 1:2], "sDB")
                bnd_dump[hh] = P.cnt("sDB")

        def emit_qstage(hh):
            emit_qmm(hh)
            emit_qfinish(hh)
            emit_qbound(hh)

        def wo_bank(g, i):
            if g == 3 and i % 2 == 1:
                return PS3[:, 512:1024]
            return PS2[:, 512:1024]

        def emit_wo_group_mm(g, i):
            dc, hf = i // 2, i % 2
            P.wait("tensor", "sWO", wo_dma[(g, 3)])
            if i == 0:
                P.wait("tensor", "sV", ohev[(4 * g + 3, 1)])
                if g == 3:
                    P.wait("tensor", "sV", qevac[NH - 1])
                    P.wait("tensor", "sA", sqrt_done[NH - 1])
            back = 2 if g == 3 else 1
            if (g, i - back) in wo_evac:
                P.wait("tensor", "sV", wo_evac[(g, i - back)])
            elif g > 0:
                P.wait("tensor", "sV", wo_evac[(g - 1, 31)])
            for k in range(4):
                P.op("tensor", "matmul", wo_bank(g, i),
                     wog[:, k * 2048 + dc * 128: k * 2048 + (dc + 1) * 128],
                     oh[g % 2][:, k * T + hf * 512: k * T + hf * 512 + 512],
                     start=(k == 0), stop=(k == 3),
                     inc="sP" if k == 3 else None)
            wo_pe[(g, i)] = P.cnt("sP")
            P.wait("vector", "sP", wo_pe[(g, i)])
            xv = x_sb[:, dc * T + hf * 512: dc * T + hf * 512 + 512]
            sc = bol[:, dc:dc + 1] if g == 0 else 0.0
            P.op("vector", "scalar_tensor_tensor", xv, wo_bank(g, i), sc, xv,
                 ALU.add, ALU.add, inc="sV")
            wo_evac[(g, i)] = P.cnt("sV")

        emit_qdma(0)
        emit_qdma(1)
        emit_qstage(0)
        emit_qstage(1)

        scratch_free = [("sV", 0)]

        def emit_sum_block(h, th):
            if dbg and th == 0 and (h - 1) in rcp_dump:
                P.wait("vector", "sDB", rcp_dump[h - 1])
            P.wait("tensor", *scratch_free[0])
            P.wait("tensor", "sA", exp_at[(h, 30 + th)])
            for TC in range(16):
                P.op("tensor", "matmul", PS2[:, 512:1024], ones128[:, :],
                     pt_full[:, TC * 1024 + th * 512: TC * 1024 + th * 512 + 512],
                     start=(TC == 0), stop=(TC == 15),
                     inc="sP" if TC == 15 else None)
            spc = P.cnt("sP")
            P.wait("vector", "sP", spc)
            P.op("vector", "tensor_copy", gs_sb[:, th * 512:th * 512 + 512],
                 PS2[:, 512:1024], inc="sV")
            M[f"sumev_{h}_{th}"] = P.cnt("sV")
            scratch_free[0] = ("sV", M[f"sumev_{h}_{th}"])

        wo_queue = []

        def emit_wo_group_mm2(g, i):
            dc, hf = i // 2, i % 2
            P.wait("tensor", "sWO", wo_dma[(g, 3)])
            P.wait("tensor", *scratch_free[0])
            if i == 0:
                P.wait("tensor", "sV", ohev[(4 * g + 3, 1)])
                if g == 3:
                    P.wait("tensor", "sV", qevac[NH - 1])
                    P.wait("tensor", "sA", sqrt_done[NH - 1])
                    P.wait("tensor", "sV", negrow_done[NH - 1])
            for k in range(4):
                P.op("tensor", "matmul", PS2[:, 512:1024],
                     wog[:, k * 2048 + dc * 128: k * 2048 + (dc + 1) * 128],
                     oh[g % 2][:, k * T + hf * 512: k * T + hf * 512 + 512],
                     start=(k == 0), stop=(k == 3),
                     inc="sP" if k == 3 else None)
            wo_pe[(g, i)] = P.cnt("sP")
            P.wait("vector", "sP", wo_pe[(g, i)])
            xv = x_sb[:, dc * T + hf * 512: dc * T + hf * 512 + 512]
            sc = bol[:, dc:dc + 1] if g == 0 else 0.0
            P.op("vector", "scalar_tensor_tensor", xv, PS2[:, 512:1024], sc, xv,
                 ALU.add, ALU.add, inc="sV")
            wo_evac[(g, i)] = P.cnt("sV")
            scratch_free[0] = ("sV", wo_evac[(g, i)])

        def emit_av_mm(h, half):
            P.wait("tensor", "sA", exp_at[(h, 30 + half)])
            if half == 1:
                P.wait("tensor", "sV", ohev[(h, 0)])
            elif (h - 1, 1) in ohev:
                P.wait("tensor", "sV", ohev[(h - 1, 1)])
            if h == 0 and half == 0:
                P.wait("tensor", "sG", M["kv_gathered"])
            for TC in range(16):
                P.op("tensor", "matmul", PS2[:, 0:512],
                     vsb[:, TC * 128:(TC + 1) * 128],
                     pt_full[:, TC * 1024 + half * 512:
                             TC * 1024 + half * 512 + 512],
                     start=(TC == 0), stop=(TC == 15),
                     inc="sP" if TC == 15 else None)
            avpe[(h, half)] = P.cnt("sP")

        def emit_ohev(h, half):
            g = h // 4
            P.wait("vector", "sP", avpe[(h, half)])
            P.wait("vector", *recip_done[h])
            if h % 4 == 0 and half == 0 and g >= 2:
                P.wait("vector", "sP", wo_pe[(g - 2, 31)])
            P.op("vector", "tensor_tensor",
                 oh[g % 2][:, (h % 4) * T + half * 512:
                           (h % 4) * T + half * 512 + 512],
                 PS2[:, 0:512], gs_sb[:, half * 512:half * 512 + 512],
                 ALU.mult, inc="sV")
            ohev[(h, half)] = P.cnt("sV")

        for h in range(NH):
            hp = h % 2
            u = 0
            for TC in range(16):
                for th in range(2):
                    bank = PSB[:, (u % 4) * 512:(u % 4) * 512 + 512]
                    ug = 32 * h + u
                    if u <= 3:
                        P.wait("tensor", "sV", negrow_done[h])
                        if h == 0:
                            P.wait("tensor", "sG", M["kv_gathered"])
                            P.wait("tensor", "sV", M["maxk2"])
                    if ug >= 4:
                        P.wait("tensor", "sA", exp_g[ug - 4])
                    P.op("tensor", "matmul", bank,
                         ksb[:, TC * 128:(TC + 1) * 128],
                         qsb[hp][:, th * 512:th * 512 + 512],
                         start=True, stop=True, inc="sP")
                    sc_pe = P.cnt("sP")
                    if u == 31:
                        M[f"score_last_{h}"] = sc_pe
                    P.wait("scalar", "sP", sc_pe)
                    if (h - 1, th) in pt_free:
                        P.wait("scalar", "sP", pt_free[(h - 1, th)])
                    P.op("scalar", "activation",
                         pt_full[:, TC * 1024 + th * 512: TC * 1024 + th * 512 + 512],
                         bank, AF.Exp, scol[:, 4 + hp:5 + hp], 1.0, inc="sA")
                    exp_at[(h, u)] = exp_g[ug] = P.cnt("sA")
                    u += 1
                    if u == 16 and h + 2 < NH:
                        emit_qdma(h + 2)
                        emit_qmm(h + 2)
                    if u == 31:
                        emit_av_mm(h, 0)
                        emit_sum_block(h, 0)
                        pt_free[(h, 0)] = P.cnt("sP")
                    if u % 4 == 3 and wo_queue:
                        emit_wo_group_mm2(*wo_queue.pop(0))
            emit_sum_block(h, 1)
            if dbg:
                P.wait("gpsimd", "sV", M[f"sumev_{h}_1"])
                P.dma("gpsimd", dbg_sum[h:h + 1, :], gs_sb[0:1, :], "sDB")
                if h == 5:
                    P.dma("gpsimd", dbg_gsf[:, :], gs_sb[:, :], "sDB")
                    P.dma("gpsimd", dbg_ones[:, :], ones128[:, :], "sDB")
                sum_dump[h] = P.cnt("sDB")
                P.wait("scalar", "sDB", sum_dump[h])
            P.wait("scalar", "sV", M[f"sumev_{h}_1"])
            P.op("scalar", "activation", recip_sb[:, :], gs_sb[:, :], AF.Ln,
                 eps2_c[:, 0:1], 1.0, inc="sA")
            P.wait("scalar", "sA", P.cnt("sA"))
            P.op("scalar", "activation", gs_sb[:, :], recip_sb[:, :], AF.Exp,
                 0.0, -1.0, inc="sA")
            recip_done[h] = ("sA", P.cnt("sA"))
            if dbg:
                P.wait("gpsimd", "sA", recip_done[h][1])
                P.dma("gpsimd", dbg_rcp[h:h + 1, :], gs_sb[0:1, :], "sDB")
                if h == 5:
                    P.dma("gpsimd", dbg_rcf[:, :], gs_sb[:, :], "sDB")
                rcp_dump[h] = P.cnt("sDB")
            if h + 2 < NH:
                emit_qfinish(h + 2)
                emit_qbound(h + 2)
            emit_ohev(h, 0)
            emit_av_mm(h, 1)
            pt_free[(h, 1)] = P.cnt("sP")
            emit_ohev(h, 1)
            if h % 4 == 3:
                g = h // 4
                for k in range(4):
                    hh = 4 * g + k
                    P.wait("gpsimd", "sP",
                           M["ffn0_pe_done"] if g == 0 else wo_pe[(g - 1, 31)])
                    P.dma("gpsimd", wog[:, k * 2048:(k + 1) * 2048],
                          wo_d[hh * 128:(hh + 1) * 128, :], "sWO")
                    wo_dma[(g, k)] = P.cnt("sWO")
                wo_queue.extend((g, i) for i in range(32))
                if g == 3:
                    while wo_queue:
                        emit_wo_group_mm2(*wo_queue.pop(0))
        M["wo_pe_done"] = P.cnt("sP")
        M["attn_x_done"] = wo_evac[(3, 31)]
        M["wo_dma_done"] = wo_dma[(3, 3)]
        M["exp_last"] = exp_at[(15, 31)]
        M["ohev_last"] = ohev[(15, 1)]

    layernorm("ln1", xb16_a)
    ffn(0, "ln1")
    if phase >= 2:
        attention()
    if phase >= 3:
        if dbg:
            P.wait("gpsimd", "sV", M["attn_x_done"])
            for c in range(DCN):
                P.dma("gpsimd", dbg_x2[c * 128:(c + 1) * 128, :],
                      x_sb[:, c * T:(c + 1) * T], "sDB")
            M["x2_dump"] = P.cnt("sDB")
        layernorm("ln3", xb16_a,
                  pre_dve=(("sWO", M["wo_dma_done"]), ("sP", M["wo_pe_done"]),
                           ("sV", M["attn_x_done"])),
                  pre_pe=(("sA", M["exp_last"]), ("sV", M["attn_x_done"])))
        if dbg:
            M["dbg_gate"] = ("sDB", P.cnt("sDB"))
        ffn(1, "ln3")
        xfin = M["ffn1_x_done"]
    elif phase == 2:
        xfin = M["attn_x_done"]
    else:
        xfin = M["ffn0_x_done"]

    P.wait("sync", "sV", xfin)
    for c in range(DCN):
        P.dma("sync", y_out[c * 128:(c + 1) * 128, :], x_sb[:, c * T:(c + 1) * T])
    P.wait("sync", "sY", P.cnt("sY"))
    if dbg:
        P.wait("sync", "sDB", P.cnt("sDB"))

    with nc.Block() as block:
        block.sync(lambda e: [f(e) for f in P.ops["sync"]])
        block.tensor(lambda e: [f(e) for f in P.ops["tensor"]])
        block.vector(lambda e: [f(e) for f in P.ops["vector"]])
        block.scalar(lambda e: [f(e) for f in P.ops["scalar"]])
        block.gpsimd(lambda e: [f(e) for f in P.ops["gpsimd"]])
    ctx.close()
    return nc


def _host_layout(inputs):
    bf = ml_dtypes.bfloat16
    f32 = np.float32
    g = lambda k: np.asarray(inputs[k])
    Wq, Wk, Wv, Wo = (np.ascontiguousarray(np.asarray(g(k), f32)) for k in
                      ("L0_attn_Wq", "L0_attn_Wk", "L0_attn_Wv", "L0_attn_Wo"))
    shared = {
        "wq": np.ascontiguousarray(
            Wq.reshape(16, 128, 16, 128).transpose(2, 1, 0, 3).reshape(D, D)),
        "wk": np.ascontiguousarray(
            Wk.reshape(16, 128, 128).transpose(1, 0, 2).reshape(128, D)),
        "wv": np.ascontiguousarray(
            Wv.reshape(16, 128, 128).transpose(1, 0, 2).reshape(128, D)),
        "wo": Wo,
        "gw": np.ascontiguousarray(
            np.asarray(g("L0_ffn_gate_w"), f32).reshape(ICN, 128, 16, 128)
            .transpose(0, 3, 2, 1).reshape(ICN * 128, D)).astype(bf),
        "uw": np.ascontiguousarray(
            np.asarray(g("L0_ffn_up_w"), f32).reshape(ICN, 128, 16, 128)
            .transpose(0, 3, 2, 1).reshape(ICN * 128, D)).astype(bf),
        "dw": np.ascontiguousarray(
            np.asarray(g("L0_ffn_down_w"), f32).T.reshape(ICN, 128, 2, 1024)
            .transpose(2, 0, 1, 3).reshape(2 * ICN * 128, 1024)).astype(bf),
        "bq_l": np.ascontiguousarray(np.asarray(g("L0_attn_bq"), f32).reshape(16, 128).T),
        "bk_c": np.ascontiguousarray(np.asarray(g("L0_attn_bk"), f32).reshape(128, 1)),
        "bv_r": np.ascontiguousarray(np.asarray(g("L0_attn_bv"), f32).reshape(1, 128)),
        "bo_l": np.ascontiguousarray(np.asarray(g("L0_attn_bo"), f32).reshape(16, 128).T),
        "gs_l": np.ascontiguousarray(np.asarray(g("L0_ffn_gate_s"), f32).reshape(ICN, 128).T),
        "us_l": np.ascontiguousarray(np.asarray(g("L0_ffn_up_s"), f32).reshape(ICN, 128).T),
        "ds_l": np.ascontiguousarray(np.asarray(g("L0_ffn_down_s"), f32).reshape(16, 128).T),
    }
    x = np.asarray(inputs["L0_attn_x"], f32)
    in_maps = []
    for c in range(NCORES):
        b, hf = c // 2, c % 2
        m = dict(shared)
        m["x_in"] = np.ascontiguousarray(x[b, hf * T:(hf + 1) * T, :].T)
        in_maps.append(m)
    return in_maps


def _run(inputs, phase=3):
    in_maps = _host_layout(inputs)
    nc = build(phase)
    res = run_bass_kernel_spmd(nc, in_maps, core_ids=list(range(NCORES)))
    y = np.empty((4, 2048, D), np.float32)
    for c, r in enumerate(res.results):
        y[c // 2, (c % 2) * T:(c % 2 + 1) * T, :] = r["y_out"].T
    return y


def kernel(**inputs):
    return _run(inputs, phase=3)
